# version 19
# baseline (speedup 1.0000x reference)
import numpy as np
import concourse.bass as bass
import concourse.mybir as mybir

F32 = mybir.dt.float32
BF16 = mybir.dt.bfloat16
AF_ = mybir.ActivationFunctionType
ALU = mybir.AluOpType
AX = mybir.AxisListType.X

D = 1024
H = 16
DH = 64
LAT = 256
FI = 2816
NDC = D // 128
NLC = LAT // 128
NIC = FI // 128
SCALE = DH ** -0.5
EPS = 1e-5
BPAD = 1.05

ENGS = ("sync", "tensor", "vector", "scalar", "gpsimd")


class Sem:
    __slots__ = ("h", "n", "name")
    def __init__(self, h, name):
        self.h, self.n, self.name = h, 0, name


class ColAlloc:
    def __init__(self, tensor, n, name):
        self.t, self.n, self.name = tensor, n, name
        self.off = 0
        self.marks = []
        self.peak = 0
    def alloc(self, cols):
        assert self.off + cols <= self.n, (
            f"{self.name}: overflow {self.off}+{cols}>{self.n}")
        ap = self.t[:, self.off:self.off + cols]
        self.off += cols
        self.peak = max(self.peak, self.off)
        return ap
    def push(self):
        self.marks.append(self.off)
    def pop(self):
        self.off = self.marks.pop()


class MK:
    def __init__(self, S=2048, taps=(), stop_after=None, p1dbg=0):
        self.stop_after = stop_after
        self.p1dbg = p1dbg
        assert S % 1024 == 0 or S in (256, 512)
        self.S = S
        self.NTC = S // 128
        self.taps = set(taps)
        self.prog = {e: [] for e in ENGS}
        self.nb = 68480
        self.nf = 15360
        self.tap_names = []

    def em(self, eng, fn, inc=None, inc_by=1):
        if inc is not None:
            inc.n += inc_by
        cell = {"upd": 1 if inc is not None else 0}
        def run(e, fn=fn, inc=inc, inc_by=inc_by, cell=cell):
            ins = fn(e)
            cell["i"] = ins
            if inc is not None:
                ins.then_inc(inc.h, inc_by)
        self.prog[eng].append(run)
        self._last_cell[eng] = cell

    def mark(self, eng, inc, inc_by=1):
        cell = None if eng == "sync" else self._last_cell.get(eng)
        if cell is None:
            self.em(eng, lambda e: e.nop(), inc=inc, inc_by=inc_by)
            return
        if cell.get("upd", 0) >= 1:
            if eng == "tensor":
                self.em(eng, lambda e: e.ldweights(self.idb[0:1, 0:1]),
                        inc=inc, inc_by=inc_by)
            elif eng == "gpsimd":
                self.em(eng, lambda e: e.tensor_copy(
                    self.cst[:, 2:3], self.cst[:, 0:1]), inc=inc, inc_by=inc_by)
            elif eng == "vector":
                self.em(eng, lambda e: e.tensor_copy(
                    self.cst[:, 3:4], self.cst[:, 0:1]), inc=inc, inc_by=inc_by)
            else:
                self.em(eng, lambda e: e.copy(
                    self.cst[:, 3:4], self.cst[:, 0:1]), inc=inc, inc_by=inc_by)
            return
        inc.n += inc_by
        cell["upd"] = cell.get("upd", 0) + 1
        def run(e, cell=cell, inc=inc, inc_by=inc_by):
            cell["i"].then_inc(inc.h, inc_by)
        self.prog[eng].append(run)

    def wait(self, eng, sem, val=None):
        v = sem.n if val is None else val
        assert v <= sem.n, f"wait on future {sem.name}: {v} > {sem.n}"
        if v <= 0 or self._lastwait.get((eng, sem.name), -1) >= v:
            return
        self._lastwait[(eng, sem.name)] = v
        self.prog[eng].append(lambda e, s=sem, v=v: e.wait_ge(s.h, v))

    def dma(self, dst, src, eng="sync"):
        s = self.sem_dma[eng]
        self.em(eng, lambda e, d=dst, x=src: e.dma_start(d, x), inc=s, inc_by=16)
        return s.n

    def barrier(self, tag=""):
        engines = ENGS
        snap = {}
        for eng in engines:
            s = self.sem_bar[eng]
            self.mark(eng, s)
            snap[eng] = s.n
        for eng in engines:
            for other in engines:
                if other != eng:
                    self.wait(eng, self.sem_bar[other], snap[other])

    def tap(self, name, ap, shape, dtype=F32, wait_sem=None):
        if name not in self.taps:
            return
        h = self.nc.dram_tensor("tap_" + name, list(shape), dtype,
                                kind="ExternalOutput").ap()
        self.tap_names.append(name)
        if wait_sem is not None:
            self.wait("sync", wait_sem)
        self.dma(h[:], ap)

    @staticmethod
    def _rng(ap):
        try:
            pitch = int(ap.tensor.shape[-1])
            lo = int(ap.offset)
            hi = lo
            for step, count in ap.ap:
                step = int(step); count = int(count)
                if abs(step) < pitch:
                    hi += (count - 1) * abs(step)
            return (ap.tensor.name, lo, hi + 1)
        except Exception:
            return (getattr(getattr(ap, "tensor", None), "name", "?"), 0, 1 << 60)

    def _chk(self, eng, ins_aps, outs_aps):
        last = self._lastout.get(eng, ())
        hit = False
        for ap in ins_aps:
            if not hasattr(ap, "tensor"):
                continue
            n, lo, hi = self._rng(ap)
            for (n2, lo2, hi2) in last:
                if n == n2 and lo < hi2 and lo2 < hi:
                    hit = True
                    break
            if hit:
                break
        if hit:
            self.nfence += 1
            if eng == "gpsimd":
                self.em(eng, lambda e: e.tensor_copy(self.cst[:, 2:3],
                                                     self.cst[:, 0:1]))
            elif eng == "vector":
                self.em(eng, lambda e: e.tensor_copy(self.cst[:, 3:4],
                                                     self.cst[:, 0:1]))
            else:
                self.em(eng, lambda e: e.copy(self.cst[:, 3:4],
                                              self.cst[:, 0:1]))
        self._lastout[eng] = tuple(self._rng(ap) for ap in outs_aps
                                   if hasattr(ap, "tensor"))

    def ts(self, eng, out, in0, s1, op0, s2=None, op1=None, inc=None):
        self._chk(eng, [in0, s1, s2], [out])
        if op1 is None:
            self.em(eng, lambda e: e.tensor_single_scalar(out, in0, s1, op0), inc=inc)
        else:
            self.em(eng, lambda e: e.tensor_scalar(out, in0, s1, s2, op0, op1), inc=inc)

    def tt(self, eng, out, in0, in1, op, inc=None):
        self._chk(eng, [in0, in1], [out])
        self.em(eng, lambda e: e.tensor_tensor(out, in0, in1, op), inc=inc)

    def stt(self, eng, out, in0, scalar, in1, op0, op1, inc=None):
        self._chk(eng, [in0, scalar, in1], [out])
        self.em(eng, lambda e: e.scalar_tensor_tensor(out, in0, scalar, in1, op0, op1),
                inc=inc)

    def cp(self, eng, out, in0, inc=None):
        self._chk(eng, [in0], [out])
        self.em(eng, lambda e: e.tensor_copy(out, in0), inc=inc)

    def acp(self, out, in0, inc=None):
        self._chk("scalar", [in0], [out])
        self.em("scalar", lambda e: e.copy(out, in0), inc=inc)

    def red(self, eng, out, in0, op, absval=False, inc=None):
        self._chk(eng, [in0], [out])
        self.em(eng, lambda e: e.tensor_reduce(out, in0, axis=AX, op=op,
                                               apply_absolute_value=absval), inc=inc)

    def rcp(self, eng, out, in0, inc=None):
        self._chk(eng, [in0], [out])
        self.em(eng, lambda e: e.reciprocal(out, in0), inc=inc)

    def mset(self, eng, ap, val, inc=None):
        self._chk(eng, [], [ap])
        self.em(eng, lambda e: e.memset(ap, val), inc=inc)

    def act(self, out, in0, func, bias=0.0, scale=1.0, inc=None):
        tgt = self.func2set[func]
        if tgt != self._cur_act_set:
            self._cur_act_set = tgt
            def load(e, tgt=tgt):
                return e.add_instruction(mybir.InstLoadActFuncSet(
                    name=e.bass.get_next_instruction_name(),
                    act_func_set_id=tgt, ins=[], outs=[]))
            self.prog["scalar"].append(lambda e, load=load: load(e))
        self._chk("scalar", [in0, bias, scale], [out])
        self.em("scalar", lambda e: e.activation(out, in0, func, bias=bias,
                                                 scale=scale), inc=inc)

    def mm(self, out, lhsT, rhs, start, stop, inc=None, tp=None, skip=False):
        self.em("tensor", lambda e: e.matmul(out, lhsT, rhs, start=start,
                                             stop=stop, tile_position=tp,
                                             skip_group_check=skip),
                inc=inc)

    def tr(self, out, in_, ident=None, inc=None):
        ident = self.idf[:] if ident is None else ident
        self.em("tensor", lambda e: e.transpose(out, in_, ident), inc=inc)

    def split_pair(self, eng, hi, lo, src, inc=None):
        self.cp(eng, hi, src)
        self.tt(eng, lo, src, hi, ALU.subtract, inc=inc)

    def penop(self, inc):
        self.mark("tensor", inc)

    def build(self):
        S, NTC = self.S, self.NTC
        nc = bass.Bass("TRN2", target_bir_lowering=False, debug=False,
                       detect_race_conditions=False)
        self.nc = nc
        self._lastwait = {}
        self._last_cell = {}
        self._lastout = {}
        self.nfence = 0
        from concourse.hw_specs import get_activation_tables
        tabs = list(get_activation_tables(nc.m.arch).items())
        self.func2set = {}
        for want, fn in (("natural_log_exp_and_others", AF_.Exp),
                         ("natural_log_exp_and_others", AF_.Ln),
                         ("sqrt_and_others", AF_.Sqrt),
                         ("silu_and_others", AF_.Silu)):
            idx = [i for i, (nm, fs) in enumerate(tabs) if nm == want]
            assert idx, want
            self.func2set[fn] = idx[0]
        self._cur_act_set = None
        dt, db = F32, BF16

        dr = {}
        def din(name, shape, d=dt):
            dr[name] = nc.dram_tensor(name, list(shape), d, kind="ExternalInput").ap()
        din("x", [S, D]); din("c", [1, D]); din("wmod", [D, 6 * D])
        din("wq", [D, D]); din("bq", [1, D]); din("wl", [D, LAT]); din("bl", [1, LAT])
        din("wk", [LAT, D]); din("bk", [1, D]); din("wv", [LAT, D]); din("bv", [1, D])
        din("wo", [D, D]); din("bo", [1, D])
        din("wg", [FI, D]); din("wu", [FI, D]); din("wd", [D, FI])
        din("cid", [128, 128]); din("cidb", [128, 128], db)
        dr["out"] = nc.dram_tensor("outp", [S, D], dt, kind="ExternalOutput").ap()
        dr["sx2"] = nc.dram_tensor("sx2", [S, D], dt, kind="Internal").ap()
        dr["sao_hi"] = nc.dram_tensor("sao_hi", [D, S], db, kind="Internal").ap()
        dr["sao_lo"] = nc.dram_tensor("sao_lo", [D, S], db, kind="Internal").ap()
        dr["swg_hi"] = nc.dram_tensor("swg_hi", [D, FI], db, kind="Internal").ap()
        dr["swg_lo"] = nc.dram_tensor("swg_lo", [D, FI], db, kind="Internal").ap()
        dr["swu_hi"] = nc.dram_tensor("swu_hi", [D, FI], db, kind="Internal").ap()
        dr["swu_lo"] = nc.dram_tensor("swu_lo", [D, FI], db, kind="Internal").ap()
        dr["swd_hi"] = nc.dram_tensor("swd_hi", [FI, D], db, kind="Internal").ap()
        dr["swd_lo"] = nc.dram_tensor("swd_lo", [FI, D], db, kind="Internal").ap()
        self.dr = dr

        from contextlib import ExitStack
        self.stack = ExitStack()
        ent = self.stack.enter_context

        self.sem_dma = {e: Sem(ent(nc.semaphore(f"dma_{e}")), f"dma_{e}")
                        for e in ("sync", "scalar", "gpsimd")}
        self.sem_bar = {e: Sem(ent(nc.semaphore(f"bar_{e}")), f"bar_{e}") for e in ENGS}
        self.sp = {e: Sem(ent(nc.semaphore(f"p_{e}")), f"p_{e}")
                   for e in ("tensor", "vector", "scalar", "gpsimd")}

        self.AB = ColAlloc(ent(nc.sbuf_tensor("AB", [128, self.nb], db)), self.nb, "AB")
        self.AF = ColAlloc(ent(nc.sbuf_tensor("AF", [128, self.nf], dt)), self.nf, "AF")
        self.idf = ent(nc.sbuf_tensor("idf", [128, 128], dt))
        self.idb = ent(nc.sbuf_tensor("idb", [128, 128], db))
        self.vec = ent(nc.sbuf_tensor("vec", [128, 128], dt))
        self.cst = ent(nc.sbuf_tensor("cst", [128, 4], dt))
        self.ones2f = ent(nc.sbuf_tensor("ones2f", [128, 2], dt))
        self.onesF = ent(nc.sbuf_tensor("onesF", [128, 64], dt))
        self.emat = ent(nc.sbuf_tensor("emat", [128, 128], dt))
        self.oncb = ent(nc.sbuf_tensor("oncb", [128, 1], db))
        self.bct = ent(nc.sbuf_tensor("bct", [128, 3 * D], dt))
        self.ps = [ent(nc.psum_tensor(f"ps{i}", [128, 512], dt)) for i in range(8)]
        self.block = ent(nc.Block())

        self.AB.push()
        self.xhT_h = self.AB.alloc(NDC * S)
        self.xhT_l = self.AB.alloc(NDC * S)
        self.latT_h = self.AB.alloc(NLC * S)
        self.latT_l = self.AB.alloc(NLC * S)
        self.saT_h = self.AB.alloc(NDC)
        self.saT_l = self.AB.alloc(NDC)
        self.onesb = self.AB.alloc(128)
        self.AB_base = self.AB.off

        sa = self.stop_after
        self.phase0()
        self.barrier("p0")
        if sa != "p0":
            self.phase1()
            self.barrier("p1")
        if sa not in ("p0", "p1"):
            self.phase23()
        self.AB.pop()
        self.barrier("p23")
        if sa is None:
            self.phase4()
            self.barrier("p4")
            self.phase6()

        self.wait("sync", self.sem_dma["sync"])
        for e in ("scalar", "gpsimd"):
            if self.sem_dma[e].n:
                self.wait(e, self.sem_dma[e])

        for eng in ENGS:
            ops = self.prog[eng]
            if not ops:
                continue
            def section(e, ops=ops):
                for f in ops:
                    f(e)
            getattr(self.block, eng)(section)

        self.stack.close()
        nc.finalize()
        return nc

    def vslot(self, i):
        return self.vec[:, i * NDC:(i + 1) * NDC]

    VCA1, VGB, VBQ, VBK, VBO, VGBD = 0, 1, 2, 3, 4, 5

    def row_to_fmajor(self, row_ap, dst, n, pcol=0):
        spe, sdv = self.sp["tensor"], self.sp["vector"]
        self.wait("tensor", sdv)
        self.wait("tensor", self.sp["scalar"])
        for c in range(n):
            self.tr(self.ps[0][:, pcol + c:pcol + c + 1],
                    row_ap[:, 128 * c:128 * (c + 1)], self.idf[0:1, 0:1])
        self.penop(spe)
        self.wait("vector", spe)
        self.cp("vector", dst, self.ps[0][:, pcol:pcol + n], inc=sdv)

    def phase0(self):
        dr, S = self.dr, self.S
        sd = self.sem_dma["sync"]
        spe, sdv, sac = self.sp["tensor"], self.sp["vector"], self.sp["scalar"]

        self.dma(self.idf[:], dr["cid"][:])
        self.dma(self.idb[:], dr["cidb"][:])
        self.mset("vector", self.onesb[0:1, :], 1.0)
        self.mset("vector", self.onesb[32:33, :], 1.0)
        self.mset("vector", self.onesF[:, :], 1.0)
        self.mset("vector", self.emat[:, :], 0.0)
        self.mset("vector", self.emat[0:1, 0:64], 1.0)
        self.mset("vector", self.emat[32:33, 64:128], 1.0)
        self.mset("vector", self.oncb[:, 0:1], 1.0)
        self.mset("vector", self.cst[:, 0:1], EPS)
        self.mset("vector", self.cst[:, 2:3], 6.0)
        self.rcp("vector", self.cst[:, 1:2], self.cst[:, 2:3])
        self.mset("vector", self.ones2f[:, :], 0.0)
        self.mset("vector", self.ones2f[0:64, 0:1], 1.0)
        self.mset("vector", self.ones2f[64:128, 1:2], 1.0, inc=sdv)

        self.AF.push()
        c_row = self.AF.alloc(1024)[0:1, :]
        sc_row = self.AF.alloc(1024)[0:1, :]
        mod_row = self.AF.alloc(6 * 1024)[0:1, :]
        tmp_row = self.AF.alloc(1024)[0:1, :]
        ones_row = self.AF.alloc(128)[0:1, :]
        scT = self.AF.alloc(NDC)
        wst = [self.AF.alloc(1536) for _ in range(2)]
        self.AB.push()
        scT_h = self.AB.alloc(NDC)
        scT_l = self.AB.alloc(NDC)
        wp_h = [self.AB.alloc(1536) for _ in range(2)]
        wp_l = [self.AB.alloc(1536) for _ in range(2)]

        self.dma(c_row, dr["c"][:])
        self.wait("scalar", sd)
        self.act(sc_row, c_row, AF_.Silu, inc=sac)
        self.mset("vector", ones_row, 1.0, inc=sdv)
        self.wait("tensor", sac)
        for c in range(NDC):
            self.tr(self.ps[0][:, c:c + 1], sc_row[:, 128 * c:128 * (c + 1)],
                    self.idf[0:1, 0:1])
        self.penop(spe)
        self.wait("vector", spe)
        self.cp("vector", scT, self.ps[0][:, 0:NDC])
        self.split_pair("vector", scT_h, scT_l, scT, inc=sdv)

        wmod_t = dr["wmod"].rearrange("(c p) n -> c p n", p=128)
        for qtr in range(4):
            c0 = qtr * 1536
            for c in range(NDC):
                buf = c % 2
                if qtr + c > 0:
                    self.wait("sync", sdv)
                self.dma(wst[buf], wmod_t[c][:, c0:c0 + 1536])
                self.wait("vector", sd)
                if qtr + c > 0:
                    self.wait("vector", spe)
                self.split_pair("vector", wp_h[buf], wp_l[buf], wst[buf], inc=sdv)
                self.wait("tensor", sdv)
                for nt in range(3):
                    pb = self.ps[2 + nt][0:1, :]
                    sl = slice(512 * nt, 512 * (nt + 1))
                    self.mm(pb, scT_h[:, c:c + 1], wp_h[buf][:, sl],
                            start=(c == 0), stop=False)
                    self.mm(pb, scT_h[:, c:c + 1], wp_l[buf][:, sl],
                            start=False, stop=False)
                    self.mm(pb, scT_l[:, c:c + 1], wp_h[buf][:, sl],
                            start=False, stop=(c == NDC - 1),
                            inc=(spe if nt == 2 else None))
            self.wait("vector", spe)
            for nt in range(3):
                self.cp("vector", mod_row[:, c0 + 512 * nt:c0 + 512 * (nt + 1)],
                        self.ps[2 + nt][0:1, :], inc=sdv)
            self.wait("tensor", sdv)
        self.tap("mod", mod_row, (1, 6 * D), wait_sem=sdv)

        sa_r = mod_row[:, 0:D]
        ca_r = mod_row[:, D:2 * D]
        ga_r = mod_row[:, 2 * D:3 * D]
        sb_r = mod_row[:, 3 * D:4 * D]
        cb_r = mod_row[:, 4 * D:5 * D]
        gb_r = mod_row[:, 5 * D:6 * D]
        for g_r in (ga_r, gb_r):
            self.ts("vector", tmp_row, g_r, 0.0, ALU.is_ge, 2.0, ALU.mult)
            self.ts("vector", tmp_row, tmp_row, 1.0, ALU.subtract)
            self.ts("vector", c_row, g_r, -1.0, ALU.mult)
            self.tt("vector", c_row, c_row, g_r, ALU.max)
            self.ts("vector", c_row, c_row, 0.25, ALU.max)
            self.tt("vector", g_r, c_row, tmp_row, ALU.mult, inc=sdv)
        self.ts("vector", cb_r, cb_r, 1.0, ALU.add)
        self.ts("vector", ca_r, ca_r, 1.0, ALU.add, inc=sdv)

        bct = self.bct
        for j, src in enumerate((cb_r, sb_r, ga_r)):
            for nt in range(D // 512):
                pb = self.ps[6 + (nt % 2)]
                self.wait("tensor", sdv)
                self.mm(pb[:], ones_row[:, 0:128], src[:, 512 * nt:512 * (nt + 1)],
                        start=True, stop=True, inc=spe)
                self.wait("vector", spe)
                self.cp("vector", bct[:, j * D + 512 * nt:j * D + 512 * (nt + 1)],
                        pb[:], inc=sdv)
        self.CB = bct[:, 0:D]
        self.SBt = bct[:, D:2 * D]
        self.GA = bct[:, 2 * D:3 * D]

        self.row_to_fmajor(ca_r, self.vslot(self.VCA1), NDC, pcol=8)
        self.row_to_fmajor(gb_r, self.vslot(self.VGB), NDC, pcol=16)
        saT = scT
        self.row_to_fmajor(sa_r, saT, NDC, pcol=24)
        self.split_pair("vector", self.saT_h, self.saT_l, saT, inc=sdv)
        bk_row = tmp_row
        self.wait("sync", sdv)
        self.dma(bk_row, dr["bk"][:])
        self.wait("vector", sd)
        self.row_to_fmajor(bk_row, self.vslot(self.VBK), NDC, pcol=32)
        bo_row = tmp_row
        self.dma(bo_row, dr["bo"][:])
        self.wait("vector", sd)
        self.tt("vector", bo_row, bo_row, ga_r, ALU.mult, inc=sdv)
        self.row_to_fmajor(bo_row, self.vslot(self.VBO), NDC, pcol=40)
        self.tap("vca1", self.vslot(self.VCA1), (128, NDC), wait_sem=sdv)
        self.tap("CB", self.CB, (128, D), wait_sem=sdv)
        self.tap("GA", self.GA, (128, D), wait_sem=sdv)
        self.AF.pop()
        self.AB.pop()

    def phase1(self):
        dr, S, NTC = self.dr, self.S, self.NTC
        sd = self.sem_dma["sync"]
        spe, sdv, sac = self.sp["tensor"], self.sp["vector"], self.sp["scalar"]
        x_t = dr["x"].rearrange("(tc p) d -> tc p d", p=128)

        self.AF.push()
        xst = [self.AF.alloc(D) for _ in range(2)]
        stats = self.AF.alloc(16)
        xh = self.AF.alloc(D)

        for tc in range(NTC):
            buf = tc % 2
            if tc >= 2:
                self.wait("sync", sdv)
            self.dma(xst[buf], x_t[tc])
            self.wait("vector", sd)
            if tc >= 1:
                self.wait("vector", spe)
            mv = stats[:, 12:14]
            self.em("vector", lambda e, x=xst[buf], o=xh: e.tensor_tensor(
                o, x, x, ALU.mult))
            self.em("vector", lambda e, x=xst[buf], m=stats[:, 0:1]: e.reduce_sum(
                m, x, axis=AX))
            self.em("vector", lambda e, o=xh, m=stats[:, 1:2]: e.reduce_sum(
                m, o, axis=AX))
            self.ts("vector", mv[:, 0:1], stats[:, 0:1], 1.0 / D, ALU.mult)
            self.ts("vector", stats[:, 2:3], stats[:, 1:2], 1.0 / D, ALU.mult)
            self.tt("vector", stats[:, 3:4], mv[:, 0:1], mv[:, 0:1], ALU.mult)
            self.tt("vector", mv[:, 1:2], stats[:, 2:3], stats[:, 3:4],
                    ALU.subtract, inc=sdv)
            s0 = stats[:, 14:15]; r0 = stats[:, 15:16]
            t1 = stats[:, 8:9]; va = stats[:, 9:10]
            self.wait("scalar", sdv)
            self.act(s0, mv[:, 1:2], AF_.Sqrt, bias=self.cst[:, 0:1], inc=sac)
            self.wait("vector", sac)
            self.ts("vector", va, mv[:, 1:2], EPS, ALU.add)
            self.rcp("vector", r0, s0)
            for _ in range(2):
                self.tt("vector", t1, r0, r0, ALU.mult)
                self.tt("vector", t1, t1, va, ALU.mult)
                self.ts("vector", t1, t1, -0.5, ALU.mult, 1.5, ALU.add)
                self.tt("vector", r0, r0, t1, ALU.mult)
            self.ts("vector", xh, xst[buf], mv[:, 0:1], ALU.subtract,
                    r0, ALU.mult, inc=sdv)
            if tc == 0:
                self.tap("xhat0", xh, (128, D), wait_sem=sdv)
            self.wait("tensor", sdv)
            if self.p1dbg < 3:
                for half in range(2):
                    pb = self.ps[4 + half]
                    if self.p1dbg >= 2:
                        continue
                    for j in range(4):
                        c = half * 4 + j
                        self.tr(pb[:, 128 * j:128 * (j + 1)],
                                xh[:, 128 * c:128 * (c + 1)])
                    self.penop(spe)
                    self.wait("vector", spe)
                    if self.p1dbg >= 1:
                        continue
                    for j in range(4):
                        c = half * 4 + j
                        sl = slice(c * S + tc * 128, c * S + (tc + 1) * 128)
                        if self.p1dbg == -1:
                            self.cp("vector", self.xhT_h[:, sl],
                                    pb[:, 128 * j:128 * (j + 1)])
                        elif self.p1dbg == -2:
                            self.split_pair("vector", self.xhT_h[:, 0:128],
                                            self.xhT_l[:, 0:128],
                                            pb[:, 128 * j:128 * (j + 1)])
                        else:
                            self.split_pair("vector", self.xhT_h[:, sl],
                                            self.xhT_l[:, sl],
                                            pb[:, 128 * j:128 * (j + 1)])
            self.mark("vector", sdv)
        self.AF.pop()
        self.tap("xhT_h", self.xhT_h[:, 0:S], (128, S), BF16, wait_sem=sdv)

    def phase23(self):
        dr, S, NTC = self.dr, self.S, self.NTC
        sd = self.sem_dma["sync"]
        spe, sdv, sac = self.sp["tensor"], self.sp["vector"], self.sp["scalar"]
        ntiles = [(o, min(512, S - o)) for o in range(0, S, 512)]

        self.AF.push()
        onesf = self.AF.alloc(128)[0:1, :]
        self.mset("vector", onesf, 1.0, inc=sdv)
        bvrow = self.AF.alloc(D)[0:1, :]
        self.AB.push()
        bv_h = self.AB.alloc(D)[0:1, :]
        bv_l = self.AB.alloc(D)[0:1, :]
        self.wait("sync", sdv)
        self.dma(bvrow, dr["bv"][:])
        self.wait("vector", sd)
        self.split_pair("vector", bv_h, bv_l, bvrow, inc=sdv)

        self.AF.push(); self.AB.push()
        wst = [self.AF.alloc(LAT) for _ in range(2)]
        wsc = self.AF.alloc(LAT)
        wu_h = self.AB.alloc(LAT); wu_l = self.AB.alloc(LAT)
        wlp_h = self.AB.alloc(NDC * LAT); wlp_l = self.AB.alloc(NDC * LAT)
        blrow = self.AF.alloc(LAT)[0:1, :]
        blv = self.AF.alloc(NLC)
        evt = self.AF.alloc(512)
        self.dma(blrow, dr["bl"][:])
        wl_t = dr["wl"].rearrange("(c p) l -> c p l", p=128)
        for dc in range(NDC):
            buf = dc % 2
            if dc >= 2:
                self.wait("sync", sdv)
            self.dma(wst[buf], wl_t[dc])
            self.wait("vector", sd)
            if dc > 0:
                self.wait("vector", spe)
            self.split_pair("vector", wu_h, wu_l, wst[buf], inc=sdv)
            self.wait("tensor", sdv)
            pb = self.ps[5][0:1, 0:LAT]
            self.mm(pb, self.saT_h[:, dc:dc + 1], wu_h, start=(dc == 0), stop=False)
            self.mm(pb, self.saT_h[:, dc:dc + 1], wu_l, start=False, stop=False)
            self.mm(pb, self.saT_l[:, dc:dc + 1], wu_h, start=False,
                    stop=(dc == NDC - 1), inc=spe)
            self.ts("vector", wsc, wst[buf], self.vslot(self.VCA1)[:, dc:dc + 1],
                    ALU.mult)
            self.split_pair("vector", wlp_h[:, dc * LAT:(dc + 1) * LAT],
                            wlp_l[:, dc * LAT:(dc + 1) * LAT], wsc, inc=sdv)
        self.wait("vector", spe)
        self.stt("vector", blrow, blrow, 0.0, self.ps[5][0:1, 0:LAT],
                 ALU.add, ALU.add, inc=sdv)
        self.row_to_fmajor(blrow, blv, NLC, pcol=0)
        for lc in range(NLC):
            for (o, n) in ntiles:
                pb = self.ps[lc][:, 0:n]
                self.wait("tensor", sdv)
                for p in range(3):
                    for dc in range(NDC):
                        lh = (wlp_h if p != 2 else wlp_l)[:, dc * LAT + lc * 128:
                                                          dc * LAT + (lc + 1) * 128]
                        rh = (self.xhT_h if p != 1 else self.xhT_l)[
                            :, dc * S + o:dc * S + o + n]
                        self.mm(pb, lh, rh, start=(p == 0 and dc == 0),
                                stop=(p == 2 and dc == NDC - 1),
                                inc=(spe if p == 2 and dc == NDC - 1 else None))
                self.wait("vector", spe)
                self.ts("vector", evt[:, 0:n], pb, blv[:, lc:lc + 1], ALU.add)
                sl = slice(lc * S + o, lc * S + o + n)
                self.split_pair("vector", self.latT_h[:, sl], self.latT_l[:, sl],
                                evt[:, 0:n], inc=sdv)
        self.AF.pop(); self.AB.pop()
        self.tap("latT_h", self.latT_h[:, 0:S], (128, S), BF16, wait_sem=sdv)

        sao_h = dr["sao_hi"].rearrange("(g p) s -> g p s", p=128)
        sao_l = dr["sao_lo"].rearrange("(g p) s -> g p s", p=128)

        GQ = 512
        ngq = S // GQ if S >= 512 else 1
        gqn = min(GQ, S)

        for g in range(H // 2):
            self.AF.push(); self.AB.push()
            self.wait("sync", spe)
            self.wait("sync", sdv)
            self.wait("sync", sac)
            self.wait("sync", sd)
            qT_h = self.AB.alloc(S); qT_l = self.AB.alloc(S)
            kT_h = self.AB.alloc(S); kT_l = self.AB.alloc(S)
            va_h = self.AB.alloc(NTC * 128); va_l = self.AB.alloc(NTC * 128)
            negB = [self.AB.alloc(S)[0:1, :] for _ in range(2)]
            self.AF.push(); self.AB.push()
            wst = [self.AF.alloc(128) for _ in range(2)]
            wsc = self.AF.alloc(128)
            wu_h = self.AB.alloc(128); wu_l = self.AB.alloc(128)
            wqp_h = self.AB.alloc(NDC * 128); wqp_l = self.AB.alloc(NDC * 128)
            bqrow = self.AF.alloc(128)[0:1, :]
            bqv = self.AF.alloc(4)
            evt = self.AF.alloc(512)
            self.wait("sync", sdv)
            self.dma(bqrow, dr["bq"][:, g * 128:(g + 1) * 128])
            for dc in range(NDC):
                buf = dc % 2
                if dc >= 2:
                    self.wait("sync", sdv)
                self.dma(wst[buf], dr["wq"].rearrange("(c p) n -> c p n", p=128)
                         [dc][:, g * 128:(g + 1) * 128])
                self.wait("vector", sd)
                if dc > 0:
                    self.wait("vector", spe)
                self.split_pair("vector", wu_h, wu_l, wst[buf], inc=sdv)
                self.wait("tensor", sdv)
                pb = self.ps[5][0:1, 0:128]
                self.mm(pb, self.saT_h[:, dc:dc + 1], wu_h, start=(dc == 0), stop=False)
                self.mm(pb, self.saT_h[:, dc:dc + 1], wu_l, start=False, stop=False)
                self.mm(pb, self.saT_l[:, dc:dc + 1], wu_h, start=False,
                        stop=(dc == NDC - 1), inc=spe)
                self.ts("vector", wsc, wst[buf], self.vslot(self.VCA1)[:, dc:dc + 1],
                        ALU.mult)
                self.split_pair("vector", wqp_h[:, dc * 128:(dc + 1) * 128],
                                wqp_l[:, dc * 128:(dc + 1) * 128], wsc, inc=sdv)
            self.wait("vector", spe)
            self.stt("vector", bqrow, bqrow, 0.0, self.ps[5][0:1, 0:128],
                     ALU.add, ALU.add, inc=sdv)
            self.row_to_fmajor(bqrow, bqv[:, 0:1], 1, pcol=0)

            qsq = self.AF.alloc(512)
            qn2 = [self.AF.alloc(S)[0:1, :] for _ in range(2)]
            norm2 = [self.AF.alloc(S)[0:1, :] for _ in range(2)]

            def norm_mms(dstrows, o, n, b0):
                self.wait("tensor", sdv)
                for hh in range(2):
                    pn = self.ps[b0 + hh][0:1, 0:n]
                    self.em("tensor", lambda e, pn=pn, q=qsq[:, 0:n], hh=hh:
                            e.matmul(pn, self.ones2f[:, hh:hh + 1], q,
                                     start=True, stop=True),
                            inc=(spe if hh else None))
                self.wait("vector", spe)
                for hh in range(2):
                    self.cp("vector", dstrows[hh][:, o:o + n],
                            self.ps[b0 + hh][0:1, 0:n],
                            inc=(sdv if hh else None))

            for ti, (o, n) in enumerate(ntiles):
                pb = self.ps[ti % 4][:, 0:n]
                self.wait("tensor", sdv)
                for p in range(3):
                    for dc in range(NDC):
                        lh = (wqp_h if p != 2 else wqp_l)[:, dc * 128:(dc + 1) * 128]
                        rh = (self.xhT_h if p != 1 else self.xhT_l)[
                            :, dc * S + o:dc * S + o + n]
                        self.mm(pb, lh, rh, start=(p == 0 and dc == 0),
                                stop=(p == 2 and dc == NDC - 1),
                                inc=(spe if p == 2 and dc == NDC - 1 else None))
                self.wait("vector", spe)
                self.ts("vector", evt[:, 0:n], pb, bqv[:, 0:1], ALU.add)
                self.split_pair("vector", qT_h[:, o:o + n], qT_l[:, o:o + n],
                                evt[:, 0:n])
                self.tt("vector", qsq[:, 0:n], qT_h[:, o:o + n], qT_h[:, o:o + n],
                        ALU.mult, inc=sdv)
                norm_mms(qn2, o, n, 4)

            wkp_h = self.AB.alloc(NLC * 128); wkp_l = self.AB.alloc(NLC * 128)
            for lc in range(NLC):
                if lc >= 1:
                    self.wait("sync", sdv)
                self.dma(wst[lc % 2], dr["wk"].rearrange("(c p) n -> c p n", p=128)
                         [lc][:, g * 128:(g + 1) * 128])
                self.wait("vector", sd)
                self.split_pair("vector", wkp_h[:, lc * 128:(lc + 1) * 128],
                                wkp_l[:, lc * 128:(lc + 1) * 128], wst[lc % 2],
                                inc=sdv)
            for ti, (o, n) in enumerate(ntiles):
                pb = self.ps[ti % 4][:, 0:n]
                self.wait("tensor", sdv)
                for p in range(3):
                    for lc in range(NLC):
                        lh = (wkp_h if p != 2 else wkp_l)[:, lc * 128:(lc + 1) * 128]
                        rh = (self.latT_h if p != 1 else self.latT_l)[
                            :, lc * S + o:lc * S + o + n]
                        self.mm(pb, lh, rh, start=(p == 0 and lc == 0),
                                stop=(p == 2 and lc == NLC - 1),
                                inc=(spe if p == 2 and lc == NLC - 1 else None))
                self.wait("vector", spe)
                self.ts("vector", evt[:, 0:n], pb,
                        self.vslot(self.VBK)[:, g:g + 1], ALU.add)
                self.split_pair("vector", kT_h[:, o:o + n], kT_l[:, o:o + n],
                                evt[:, 0:n])
                self.tt("vector", qsq[:, 0:n], kT_h[:, o:o + n], kT_h[:, o:o + n],
                        ALU.mult, inc=sdv)
                norm_mms(norm2, o, n, 6)
            if g == 0:
                self.tap("kn20", norm2[0], (1, S), F32, wait_sem=sdv)
                self.tap("kn21", norm2[1], (1, S), F32, wait_sem=sdv)
            knm2a = self.AF.alloc(4)
            for hh in range(2):
                km = knm2a[0:1, hh:hh + 1]
                self.red("vector", km, norm2[hh], ALU.max)
                self.ts("vector", norm2[hh], qn2[hh], km, ALU.mult, inc=sdv)
                self.wait("scalar", sdv)
                self.act(norm2[hh], norm2[hh], AF_.Sqrt, inc=sac)
                self.wait("vector", sac)
                self.ts("vector", negB[hh], norm2[hh],
                        -(BPAD * BPAD), ALU.mult, inc=sdv)

            if g == 0:
                self.tap("qn20", qn2[0], (1, S), F32, wait_sem=sdv)
                self.tap("qn21", qn2[1], (1, S), F32, wait_sem=sdv)
                self.tap("knm", knm2a[0:1, 0:2], (1, 2), F32, wait_sem=sdv)
            wvp_h = wkp_h; wvp_l = wkp_l
            for lc in range(NLC):
                self.wait("sync", sdv)
                self.dma(wst[lc % 2], dr["wv"].rearrange("(c p) n -> c p n", p=128)
                         [lc][:, g * 128:(g + 1) * 128])
                self.wait("vector", sd)
                self.split_pair("vector", wvp_h[:, lc * 128:(lc + 1) * 128],
                                wvp_l[:, lc * 128:(lc + 1) * 128], wst[lc % 2],
                                inc=sdv)
            for tc in range(NTC):
                pb = self.ps[4 + (tc % 2)][:, 0:128]
                self.wait("tensor", sdv)
                for p in range(3):
                    for lc in range(NLC):
                        lh = (self.latT_h if p != 1 else self.latT_l)[
                            :, lc * S + tc * 128:lc * S + (tc + 1) * 128]
                        rh = (wvp_h if p != 2 else wvp_l)[:, lc * 128:(lc + 1) * 128]
                        self.mm(pb, lh, rh, start=(p == 0 and lc == 0), stop=False)
                bsl = slice(g * 128, g * 128 + 128)
                self.mm(pb, self.onesb[0:1, :], bv_h[:, bsl], start=False, stop=False)
                self.mm(pb, self.onesb[0:1, :], bv_l[:, bsl], start=False, stop=True,
                        inc=spe)
                self.wait("vector", spe)
                self.split_pair("vector", va_h[:, tc * 128:(tc + 1) * 128],
                                va_l[:, tc * 128:(tc + 1) * 128], pb, inc=sdv)

            self.AF.pop(); self.AB.pop()
            self.AF.push(); self.AB.push()
            pf4 = [self.AF.alloc(512) for _ in range(4)]
            ph6 = [self.AB.alloc(512) for _ in range(6)]
            pl6 = [self.AB.alloc(512) for _ in range(6)]
            rrow = self.AF.alloc(512)
            self.mset("vector", rrow[0:33, :], 0.0, inc=sdv)
            self.wait("tensor", sdv)
            rbb = self.AF.alloc(512)
            nf32 = self.AF.alloc(512)
            ao2 = [(self.AB.alloc(512), self.AB.alloc(512)) for _ in range(2)]
            NK = ngq * NTC
            exp_f = {}; spl_f = {}; av_f = {}
            tail_exp = {}; tail_r1 = {}; tail_tt = {}; tail_ln = {}
            ao_dma = {}

            def emit_st(K):
                gq, kjc = K // NTC, K % NTC
                o0 = gq * gqn
                nq = gqn
                sb = [self.ps[2 * (K % 2) + hh][:, 0:nq] for hh in range(2)]
                if K >= 2:
                    self.wait("tensor", sac, exp_f[K - 2])
                for p in range(3):
                    for hh in range(2):
                        hp = hh * 64
                        lh = (kT_h if p != 2 else kT_l)[
                            hp:hp + 64, kjc * 128:(kjc + 1) * 128]
                        rh = (qT_h if p != 1 else qT_l)[hp:hp + 64, o0:o0 + nq]
                        self.mm(sb[hh], lh, rh, start=(p == 0), stop=False)
                self.mm(sb[0], self.onesb[0:1, :], negB[0][:, o0:o0 + nq],
                        start=False, stop=True)
                self.mm(sb[1], self.onesb[0:1, :], negB[1][:, o0:o0 + nq],
                        start=False, stop=True, inc=spe)
                pe_n = spe.n
                self.wait("scalar", spe, pe_n)
                if K >= 2:
                    self.wait("scalar", sdv, spl_f[K - 2])
                for hh in range(2):
                    self.act(pf4[(K % 2) * 2 + hh][:, 0:nq], sb[hh], AF_.Exp,
                             scale=SCALE, inc=(sac if hh else None))
                exp_f[K] = sac.n
                self.wait("vector", sac, exp_f[K])
                if K >= 3:
                    self.wait("vector", spe, av_f[K - 3])
                for hh in range(2):
                    self.cp("vector", ph6[(K % 3) * 2 + hh][:, 0:nq],
                            pf4[(K % 2) * 2 + hh][:, 0:nq])
                for hh in range(2):
                    self.tt("vector", pl6[(K % 3) * 2 + hh][:, 0:nq],
                            pf4[(K % 2) * 2 + hh][:, 0:nq],
                            ph6[(K % 3) * 2 + hh][:, 0:nq], ALU.subtract,
                            inc=(sdv if hh else None))
                spl_f[K] = sdv.n

            def emit_av(K):
                gq, kjc = K // NTC, K % NTC
                nq = gqn
                ob = self.ps[4 + gq % 2]
                sm = self.ps[6]
                self.wait("tensor", sdv, spl_f[K])
                if kjc == 0:
                    self.wait("tensor", sdv, tail_tt.get(gq - 2, 0))
                    self.wait("tensor", sac, tail_ln.get(gq - 1, 0))
                vh = [va_h[:, kjc * 128 + hh * 64:kjc * 128 + hh * 64 + 64]
                      for hh in range(2)]
                vl = [va_l[:, kjc * 128 + hh * 64:kjc * 128 + hh * 64 + 64]
                      for hh in range(2)]
                P = [ph6[(K % 3) * 2 + hh][:, 0:nq] for hh in range(2)]
                L = [pl6[(K % 3) * 2 + hh][:, 0:nq] for hh in range(2)]
                for hh in range(2):
                    self.mm(ob[64 * hh:64 * hh + 64, 0:nq], vh[hh], P[hh],
                            start=(kjc == 0), stop=False,
                            tp=(0, 64 * hh), skip=True)
                for hh in range(2):
                    self.mm(sm[32 * hh:32 * hh + 1, 0:nq], self.oncb[:, 0:1],
                            P[hh], start=(kjc == 0), stop=False,
                            tp=(0, 32 * hh), skip=True)
                for hh in range(2):
                    self.mm(ob[64 * hh:64 * hh + 64, 0:nq], vh[hh], L[hh],
                            start=False, stop=False, tp=(0, 64 * hh), skip=True)
                for hh in range(2):
                    self.mm(sm[32 * hh:32 * hh + 1, 0:nq], self.oncb[:, 0:1],
                            L[hh], start=False, stop=(kjc == NTC - 1),
                            tp=(0, 32 * hh), skip=True)
                for hh in range(2):
                    self.mm(ob[64 * hh:64 * hh + 64, 0:nq], vl[hh], P[hh],
                            start=False, stop=(kjc == NTC - 1), tp=(0, 64 * hh),
                            skip=True, inc=(spe if hh else None))
                av_f[K] = spe.n
                if kjc == NTC - 1:
                    self.wait("scalar", spe, av_f[K])
                    self.wait("scalar", spe, tail_r1.get(gq - 1, 0))
                    for hh in range(2):
                        self.act(rrow[32 * hh:32 * hh + 1, 0:nq],
                                 sm[32 * hh:32 * hh + 1, 0:nq], AF_.Ln,
                                 inc=(sac if hh else None))
                    tail_ln[gq] = sac.n
                    for hh in range(2):
                        self.act(rrow[32 * hh:32 * hh + 1, 0:nq],
                                 rrow[32 * hh:32 * hh + 1, 0:nq], AF_.Exp,
                                 scale=-1.0, inc=(sac if hh else None))
                    tail_exp[gq] = sac.n

            def emit_tail_pe(gq):
                nq = gqn
                o0 = gq * gqn
                pr = self.ps[7]
                self.wait("tensor", sac, tail_exp[gq])
                self.wait("tensor", sdv, tail_tt.get(gq - 1, 0))
                self.em("tensor", lambda e: e.matmul(
                    pr[:, 0:nq], self.emat[0:33, :], rrow[0:33, 0:nq],
                    start=True, stop=True), inc=spe)
                tail_r1[gq] = spe.n
                aoh, aol = ao2[gq % 2]
                ob = self.ps[4 + gq % 2]
                self.wait("vector", spe, tail_r1[gq])
                self.wait("vector", sd, ao_dma.get(gq - 2, 0))
                self.cp("vector", rbb[:, 0:nq], pr[:, 0:nq])
                self.tt("vector", nf32[:, 0:nq], ob[:, 0:nq], rbb[:, 0:nq],
                        ALU.mult)
                self.split_pair("vector", aoh[:, 0:nq], aol[:, 0:nq],
                                nf32[:, 0:nq], inc=sdv)
                tail_tt[gq] = sdv.n
                self.wait("sync", sdv, tail_tt[gq])
                r0_ = g * 128
                self.dma(dr["sao_hi"][r0_:r0_ + 128, o0:o0 + nq], aoh[:, 0:nq])
                self.dma(dr["sao_lo"][r0_:r0_ + 128, o0:o0 + nq], aol[:, 0:nq])
                ao_dma[gq] = self.sem_dma["sync"].n

            for K in range(NK + 2):
                if K < NK:
                    emit_st(K)
                if K - 2 >= 0:
                    Kav = K - 2
                    emit_av(Kav)
                    gq, kjc = Kav // NTC, Kav % NTC
                    if kjc == NTC - 1:
                        emit_tail_pe(gq)
            if g == 0:
                self.tap("q0T_h", qT_h, (128, S), BF16, wait_sem=sdv)
                self.tap("negB0", negB[0], (1, S), BF16, wait_sem=sdv)
                self.tap("negB1", negB[1], (1, S), BF16, wait_sem=sdv)
            self.AF.pop(); self.AB.pop()
            self.AF.pop(); self.AB.pop()
        self.AF.pop(); self.AB.pop()

    def phase4(self):
        dr, S, NTC = self.dr, self.S, self.NTC
        sd = self.sem_dma["sync"]
        spe, sdv, sac = self.sp["tensor"], self.sp["vector"], self.sp["scalar"]

        self.AF.push(); self.AB.push()
        wst = [self.AF.alloc(D) for _ in range(2)]
        wsc = self.AF.alloc(D)
        woP_h = self.AB.alloc(NDC * D); woP_l = self.AB.alloc(NDC * D)
        wo_t = dr["wo"].rearrange("(c p) n -> c p n", p=128)
        for dc in range(NDC):
            buf = dc % 2
            if dc >= 2:
                self.wait("sync", sdv)
            self.dma(wst[buf], wo_t[dc])
            self.wait("vector", sd)
            self.tt("vector", wsc, wst[buf], self.GA, ALU.mult)
            self.split_pair("vector", woP_h[:, dc * D:(dc + 1) * D],
                            woP_l[:, dc * D:(dc + 1) * D], wsc, inc=sdv)
        ao_h = self.AB.alloc(NDC * S); ao_l = self.AB.alloc(NDC * S)
        sao_h = dr["sao_hi"].rearrange("(g p) s -> g p s", p=128)
        sao_l = dr["sao_lo"].rearrange("(g p) s -> g p s", p=128)
        for dc in range(NDC):
            self.dma(ao_h[:, dc * S:(dc + 1) * S], sao_h[dc])
            self.dma(ao_l[:, dc * S:(dc + 1) * S], sao_l[dc])
        resh = self.AF.alloc(NDC * (S // 2))
        x2t = [self.AF.alloc(D) for _ in range(2)]
        HS = S // 2
        x_t = dr["x"].rearrange("(tc p) d -> tc p d", p=128)
        x2_t = dr["sx2"].rearrange("(tc p) d -> tc p d", p=128)
        self.wait("tensor", sd)
        self.wait("tensor", sdv)
        htiles = [(o, min(512, HS - o)) for o in range(0, HS, 512)]
        for half in range(2):
            o0 = half * HS
            for doutc in range(NDC):
                for ti, (o, n) in enumerate(htiles):
                    pb = self.ps[ti % 4][:, 0:n]
                    self.wait("tensor", sdv)
                    for p in range(3):
                        for dc in range(NDC):
                            lh = (woP_h if p != 2 else woP_l)[
                                :, dc * D + doutc * 128:dc * D + (doutc + 1) * 128]
                            rh = (ao_h if p != 1 else ao_l)[
                                :, dc * S + o0 + o:dc * S + o0 + o + n]
                            self.mm(pb, lh, rh, start=(p == 0 and dc == 0),
                                    stop=(p == 2 and dc == NDC - 1),
                                    inc=(spe if p == 2 and dc == NDC - 1 else None))
                    self.wait("vector", spe)
                    self.ts("vector", resh[:, doutc * HS + o:doutc * HS + o + n], pb,
                            self.vslot(self.VBO)[:, doutc:doutc + 1], ALU.add,
                            inc=sdv)
            for j in range(HS // 128):
                tc = half * (HS // 128) + j
                buf = j % 2
                if j >= 2:
                    self.wait("sync", sdv)
                self.dma(x2t[buf], x_t[tc])
                self.wait("tensor", sdv)
                for grp in range(2):
                    pb = self.ps[4 + grp]
                    for jj in range(4):
                        dc = grp * 4 + jj
                        self.tr(pb[:, jj * 128:(jj + 1) * 128],
                                resh[:, dc * HS + j * 128:dc * HS + (j + 1) * 128])
                    self.penop(spe)
                    self.wait("vector", spe)
                    self.wait("vector", sd)
                    for jj in range(4):
                        dc = grp * 4 + jj
                        dsl = slice(dc * 128, (dc + 1) * 128)
                        self.tt("vector", x2t[buf][:, dsl], x2t[buf][:, dsl],
                                pb[:, jj * 128:(jj + 1) * 128], ALU.add)
                self.mark("vector", sdv)
                if tc == 0:
                    self.tap("x2_0", x2t[buf], (128, D), wait_sem=sdv)
                self.wait("sync", sdv)
                self.dma(x2_t[tc], x2t[buf])
            if half == 0:
                self.wait("tensor", sdv)
        self.AF.pop(); self.AB.pop()

    def quant_pair_zip(self, srcs, bufs, inc=None, waits=()):
        (srcA, wA), (srcB, wB) = srcs
        an, aL, aH, aU, s3 = bufs[:5]
        for sem, val in waits:
            self.wait("vector", sem, val)
        nbA, nbB = wA // 16, wB // 16
        Z = []
        def emit(lane, fn):
            Z.append((lane, fn))
        for lane, (srcX, wX, nbX, o) in enumerate(((srcA, wA, nbA, 0),
                                                   (srcB, wB, nbB, 1))):
            v3 = lambda ap, w=wX, c0=lane * 512: ap[:, c0:c0 + w].rearrange(
                "p (nb b) -> p nb b", b=16)
            sl = lambda ap, w=wX, c0=lane * 512: ap[:, c0:c0 + w]
            amax = s3[:, (3 * lane) * 64:(3 * lane) * 64 + nbX]
            s_ = s3[:, (3 * lane + 1) * 64:(3 * lane + 1) * 64 + nbX]
            rs = s3[:, (3 * lane + 2) * 64:(3 * lane + 2) * 64 + nbX]
            r2b = rs.rearrange("p (nb b) -> p nb b", b=1).broadcast_to(
                [128, nbX, 16])
            sb_ = s_.rearrange("p (nb b) -> p nb b", b=1).broadcast_to(
                [128, nbX, 16])
            vsrc = srcX.rearrange("p (nb b) -> p nb b", b=16)
            emit(lane, lambda v, a=amax, s=vsrc: self.red(
                "vector", a, s, ALU.max, absval=True))
            emit(lane, lambda v, s=s_, a=amax: self.ts(
                "vector", s, a, self.cst[:, 1:2], ALU.mult))
            emit(lane, lambda v, s=s_: self.ts(
                "vector", s, s, 1e-12, ALU.max))
            emit(lane, lambda v, r=rs, s=s_: self.rcp("vector", r, s))
            emit(lane, lambda v, o=sl(aL), s=srcX: self.tt(
                "vector", o, s, s, ALU.mult))
            emit(lane, lambda v, r=rs: self.tt("vector", r, r, r, ALU.mult))
            emit(lane, lambda v, o=v3(an), i=v3(aL), r=r2b: self.tt(
                "vector", o, i, r, ALU.mult))
            emit(lane, lambda v, o=sl(aU), s=srcX: self.ts(
                "vector", o, s, 0.0, ALU.is_gt, 0.5, ALU.subtract))
            emit(lane, lambda v, o=sl(aL), A=sl(an): self.ts(
                "vector", o, A, 0.0625, ALU.is_gt))
            emit(lane, lambda v, o=sl(aH), A=sl(an): self.ts(
                "vector", o, A, 25.0, ALU.is_gt, 2.0, ALU.mult))
            for c, tgt in ((0.5625, "L"), (12.25, "H"), (1.5625, "L"),
                           (6.25, "H"), (3.0625, "L")):
                b = sl(aL) if tgt == "L" else sl(aH)
                emit(lane, lambda v, o=b, A=sl(an), c=c: self.stt(
                    "vector", o, A, c, o, ALU.is_gt, ALU.add))
            emit(lane, lambda v, o=sl(aH), l=sl(aL): self.stt(
                "vector", o, o, 2.0, l, ALU.mult, ALU.add))
            emit(lane, lambda v, o=sl(aH), u=sl(aU): self.tt(
                "vector", o, o, u, ALU.mult))
            emit(lane, lambda v, o=vsrc, h=v3(aH), s=sb_: self.tt(
                "vector", o, h, s, ALU.mult))
        A = [f for l, f in Z if l == 0]
        B = [f for l, f in Z if l == 1]
        for k in range(max(len(A), len(B))):
            if k < len(A):
                A[k](None)
            if k < len(B):
                B[k](None)
        if inc is not None:
            self.mark("vector", inc)
        return self.sp["gpsimd"].n


    def quant_pair(self, srcs, bufs, inc=None, waits=()):
        if not getattr(self, "qp_pool", False):
            return self.quant_pair_zip(srcs, bufs, inc=inc, waits=waits)
        return self.quant_pair_pool(srcs, bufs, inc=inc, waits=waits)

    def quant_pair_pool(self, srcs, bufs, inc=None, waits=()):
        (srcA, wA), (srcB, wB) = srcs
        an, aL, aH, aU, s3, sc2 = bufs
        sdv, sgp = self.sp["vector"], self.sp["gpsimd"]
        nbA, nbB = wA // 16, wB // 16
        V = "vector"
        G = "gpsimd" if getattr(self, "qp_pool", True) else "vector"
        entry_gp = sgp.n
        for sem, val in waits:
            self.wait(V, sem, val)
            self.wait(G, sem, val)
        self.wait(V, sgp, entry_gp)
        if getattr(self, "qp_serial", False) and G == "gpsimd":
            for e_ in ("vector", "tensor", "scalar"):
                s_e = self.sp[e_]
                if s_e.n:
                    self.wait(G, s_e, s_e.n)

        def mk(lane, srcX, wX, nbX):
            c0 = lane * 512
            sl = lambda ap, w=wX, c0=c0: ap[:, c0:c0 + w]
            v3 = lambda ap, w=wX, c0=c0: ap[:, c0:c0 + w].rearrange(
                "p (nb b) -> p nb b", b=16)
            amax = s3[:, (3 * lane) * 64:(3 * lane) * 64 + nbX]
            s_ = s3[:, (3 * lane + 1) * 64:(3 * lane + 1) * 64 + nbX]
            rs = s3[:, (3 * lane + 2) * 64:(3 * lane + 2) * 64 + nbX]
            bs = lambda ap, nbX=nbX: ap.rearrange(
                "p (nb b) -> p nb b", b=1).broadcast_to([128, nbX, 16])
            vsrc = srcX.rearrange("p (nb b) -> p nb b", b=16)
            return sl, v3, amax, s_, rs, bs, vsrc
        slA, v3A, amA, sA, rsA, bsA, vsA = mk(0, srcA, wA, nbA)
        slB, v3B, amB, sB, rsB, bsB, vsB = mk(1, srcB, wB, nbB)

        self.red(V, amA, vsA, ALU.max, absval=True)
        self.red(V, amB, vsB, ALU.max, absval=True)
        self.ts(V, sA, amA, self.cst[:, 1:2], ALU.mult)
        self.ts(V, sB, amB, self.cst[:, 1:2], ALU.mult)
        self.ts(V, sA, sA, 1e-12, ALU.max)
        self.ts(V, sB, sB, 1e-12, ALU.max)
        self.rcp(V, rsA, sA)
        self.rcp(V, rsB, sB)
        self.tt(V, rsA, rsA, rsA, ALU.mult)
        self.tt(V, rsB, rsB, rsB, ALU.mult)
        self.tt(V, slB(an), srcB, srcB, ALU.mult)
        self.tt(V, v3B(an), v3B(an), bsB(rsB), ALU.mult, inc=sdv)
        hb = sdv.n
        self.wait(G, sdv, hb)
        self.ts(G, slB(aL), slB(an), 0.0625, ALU.is_gt)
        self.ts(G, slB(aH), slB(an), 25.0, ALU.is_gt, 2.0, ALU.mult)
        self.ts(G, slB(aU), srcB, 0.0, ALU.is_gt, 0.5, ALU.subtract)
        stepsB = ((0.5625, aL), (12.25, aH), (1.5625, aL), (6.25, aH),
                  (3.0625, aL))
        for k, (c, acc) in enumerate(stepsB):
            scv = sc2[:, (k % 2) * 512:(k % 2) * 512 + wB]
            self.ts(G, scv, slB(an), c, ALU.is_gt)
            if k >= 1:
                _, accp = stepsB[k - 1]
                self.tt(G, slB(accp), slB(accp),
                        sc2[:, ((k - 1) % 2) * 512:((k - 1) % 2) * 512 + wB],
                        ALU.add)
        _, accl = stepsB[-1]
        kl = (len(stepsB) - 1) % 2
        self.tt(G, slB(accl), slB(accl), sc2[:, kl * 512:kl * 512 + wB],
                ALU.add)
        self.ts(G, slB(aH), slB(aH), 2.0, ALU.mult)
        self.tt(G, slB(aH), slB(aH), slB(aL), ALU.add)
        self.tt(G, slB(aH), slB(aH), slB(aU), ALU.mult, inc=sgp)
        gp_tail = sgp.n
        self.tt(V, slA(an), srcA, srcA, ALU.mult)
        self.ts(V, slA(aU), srcA, 0.0, ALU.is_gt, 0.5, ALU.subtract)
        self.tt(V, v3A(an), v3A(an), bsA(rsA), ALU.mult)
        self.ts(V, slA(aL), slA(an), 0.0625, ALU.is_gt)
        self.ts(V, slA(aH), slA(an), 25.0, ALU.is_gt, 2.0, ALU.mult)
        for c, tgt in ((0.5625, "L"), (12.25, "H"), (1.5625, "L"),
                       (6.25, "H"), (3.0625, "L")):
            b = slA(aL) if tgt == "L" else slA(aH)
            self.stt(V, b, slA(an), c, b, ALU.is_gt, ALU.add)
        self.stt(V, slA(aH), slA(aH), 2.0, slA(aL), ALU.mult, ALU.add)
        self.tt(V, slA(aH), slA(aH), slA(aU), ALU.mult)
        self.tt(V, vsA, v3A(aH), bsA(sA), ALU.mult)
        self.wait(V, sgp, gp_tail)
        self.tt(V, vsB, v3B(aH), bsB(sB), ALU.mult, inc=inc)
        if getattr(self, "qp_serial", False) and G == "gpsimd":
            for e_ in ("vector", "tensor", "scalar"):
                self.wait(e_, sgp, sgp.n)
        return sgp.n

    def tr_split(self, src, nct, dst_h, dst_l, stride, coff, pe_wait=None,
                 dve_wait=None):
        spe, sdv = self.sp["tensor"], self.sp["vector"]
        self.wait("tensor", sdv, pe_wait)
        groups = []
        done = 0
        while done < nct:
            g = min(4, nct - done)
            pb = self.ps[4 + len(groups) % 2]
            if len(groups) >= 2:
                self.wait("tensor", sdv)
            for jj in range(g):
                c = done + jj
                self.tr(pb[:, jj * 128:(jj + 1) * 128],
                        src[:, c * 128:(c + 1) * 128])
            self.penop(spe)
            groups.append((pb, done, g))
            done += g
            if len(groups) == 2 or done >= nct:
                sac = self.sp["scalar"]
                self.wait("scalar", spe)
                if dve_wait is not None:
                    self.wait("scalar", dve_wait[0], dve_wait[1])
                views = []
                for (pbx, d0, gx) in groups:
                    hv = dst_h.rearrange("p (c t) -> p c t", t=stride)[
                        :, d0:d0 + gx, coff:coff + 128]
                    lv = dst_l.rearrange("p (c t) -> p c t", t=stride)[
                        :, d0:d0 + gx, coff:coff + 128]
                    pv = pbx[:, 0:gx * 128].rearrange("p (c t) -> p c t", t=128)
                    views.append((hv, lv, pv))
                for hv, lv, pv in views:
                    self.acp(hv, pv)
                self.mark("scalar", sac)
                self.wait("vector", sac)
                for hv, lv, pv in views:
                    self.tt("vector", lv, pv, hv, ALU.subtract)
                self.mark("vector", sdv)
                groups = []
        self.wait("tensor", sdv)
        return sdv.n

    def phase6(self):
        dr, S, NTC = self.dr, self.S, self.NTC
        sd = self.sem_dma["sync"]
        spe, sdv, sac = self.sp["tensor"], self.sp["vector"], self.sp["scalar"]
        sgp = self.sp["gpsimd"]
        TQ = min(512, S)
        NQ = S // TQ
        NTQ = TQ // 128
        itiles = [(o, min(512, FI - o)) for o in range(0, FI, 512)]

        self.AF.push(); self.AB.push()
        hqT_h = self.AB.alloc(NDC * TQ); hqT_l = self.AB.alloc(NDC * TQ)
        h2T_h = self.AB.alloc(NIC * TQ); h2T_l = self.AB.alloc(NIC * TQ)
        wsg_h = self.AB.alloc(NDC * 512); wsg_l = self.AB.alloc(NDC * 512)
        wsu_h = self.AB.alloc(NDC * 512); wsu_l = self.AB.alloc(NDC * 512)
        dslot_h = [self.AB.alloc(D) for _ in range(2)]
        dslot_l = [self.AB.alloc(D) for _ in range(2)]
        sp_h = self.AB.alloc(D); sp_l = self.AB.alloc(D)
        ev_h = self.AB.alloc(512); ev_l = self.AB.alloc(512)
        yT = self.AF.alloc(NDC * TQ)
        x2t = [self.AF.alloc(D) for _ in range(2)]
        wst = self.AF.alloc(D)
        wst2 = [wst, x2t[0]]
        h2t2 = [self.AF.alloc(512) for _ in range(2)]
        sg2 = [self.AF.alloc(512) for _ in range(2)]
        su2 = [self.AF.alloc(512) for _ in range(2)]
        an = self.AF.alloc(1024); aL = self.AF.alloc(1024); aH = self.AF.alloc(1024)
        s3 = self.AF.alloc(6 * 64)
        stats = self.AF.alloc(20)
        aU = self.AF.alloc(1024)
        sc2 = yT[:, 0:1024]
        qb = (an, aL, aH, aU, s3, sc2)

        x2_t = dr["sx2"].rearrange("(tc p) d -> tc p d", p=128)
        out_t = dr["out"].rearrange("(tc p) d -> tc p d", p=128)
        wg_t = dr["wg"].rearrange("(c p) n -> c p n", p=128)
        wu_t = dr["wu"].rearrange("(c p) n -> c p n", p=128)
        wd_t = dr["wd"].rearrange("(c p) n -> c p n", p=128)
        swg_h = dr["swg_hi"].rearrange("(c p) n -> c p n", p=128)
        swg_l = dr["swg_lo"].rearrange("(c p) n -> c p n", p=128)
        swu_h = dr["swu_hi"].rearrange("(c p) n -> c p n", p=128)
        swu_l = dr["swu_lo"].rearrange("(c p) n -> c p n", p=128)
        swd_h = dr["swd_hi"].rearrange("(c p) n -> c p n", p=128)
        swd_l = dr["swd_lo"].rearrange("(c p) n -> c p n", p=128)

        wd_jobs = [(dc, o, n) for dc in range(NDC) for (o, n) in itiles]
        wd_st = {"k": 0, "th": {}, "ev": {}}
        bs_done = [0]

        def wd_job(dc, o, n):
            k = wd_st["k"]; wd_st["k"] += 1
            buf = wst2[k % 2]
            self.wait("sync", sdv, wd_st["th"].get(k - 2, 0))
            dmae = self.dma(buf[:, 0:n], wd_t[dc][:, o:o + n])
            hn = (n // 32) * 16
            gq = self.quant_pair(((buf[:, 0:hn], hn), (buf[:, hn:n], n - hn)),
                                 qb, inc=sdv, waits=((sd, dmae),))
            qh = sdv.n
            self.wait("tensor", sdv, qh)
            self.wait("tensor", sgp, gq)
            self.wait("scalar", self.sem_dma["scalar"],
                      wd_st["ev"].get(k - 1, 0))
            for j in range(n // 128):
                pb = self.ps[4 + (j % 2)]
                if j >= 2:
                    self.wait("tensor", sdv)
                self.tr(pb[:, 0:128], buf[:, j * 128:(j + 1) * 128])
                self.mark("tensor", spe)
                self.wait("scalar", spe)
                self.acp(ev_h[:, j * 128:(j + 1) * 128], pb[:, 0:128])
                self.mark("scalar", sac)
                self.wait("vector", sac)
                self.tt("vector", ev_l[:, j * 128:(j + 1) * 128], pb[:, 0:128],
                        ev_h[:, j * 128:(j + 1) * 128], ALU.subtract, inc=sdv)
            nc_ = n // 128
            c0_ = o // 128
            dsth = swd_h[c0_:c0_ + nc_][:, :, dc * 128:(dc + 1) * 128].rearrange(
                "c p d -> p c d")
            dstl = swd_l[c0_:c0_ + nc_][:, :, dc * 128:(dc + 1) * 128].rearrange(
                "c p d -> p c d")
            srch = ev_h[:, 0:n].rearrange("p (c d) -> p c d", d=128)
            srcl = ev_l[:, 0:n].rearrange("p (c d) -> p c d", d=128)
            self.wait("scalar", sdv)
            self.dma(dsth, srch, eng="scalar")
            wd_st["ev"][k] = self.dma(dstl, srcl, eng="scalar")
            wd_st["th"][k] = sdv.n

        def build_slots(o, n):
            nct = n // 128
            jobs = []
            for jc in range(nct):
                jobs.append((wg_t, wsg_h, wsg_l, jc))
                jobs.append((wu_t, wsu_h, wsu_l, jc))
            qh = {}
            th = {}
            for k, (wt, dsth, dstl, jc) in enumerate(jobs):
                ic = o // 128 + jc
                buf = wst2[k % 2]
                if k >= 2:
                    self.wait("sync", sdv, th[k - 2])
                dmae = self.dma(buf, wt[ic])
                gq = self.quant_pair(((buf[:, 0:512], 512),
                                      (buf[:, 512:1024], 512)),
                                     qb, inc=sdv, waits=((sd, dmae),))
                qh[k] = sdv.n
                self.wait("tensor", sgp, gq)
                th[k] = self.tr_split(buf, NDC, dsth, dstl, 512, jc * 128,
                                      pe_wait=qh[k],
                                      dve_wait=(self.sem_dma["scalar"],
                                                bs_done[0]))
            self.wait("scalar", sdv, th[len(jobs) - 1])
            for dc in range(NDC):
                self.dma(swg_h[dc][:, o:o + n], wsg_h[:, dc * 512:dc * 512 + n],
                         eng="scalar")
                self.dma(swg_l[dc][:, o:o + n], wsg_l[:, dc * 512:dc * 512 + n],
                         eng="scalar")
                self.dma(swu_h[dc][:, o:o + n], wsu_h[:, dc * 512:dc * 512 + n],
                         eng="scalar")
                self.dma(swu_l[dc][:, o:o + n], wsu_l[:, dc * 512:dc * 512 + n],
                         eng="scalar")
            bs_done[0] = self.sem_dma["scalar"].n

        def load_wslot(sh, sl_, dsth, dstl, o, n):
            self.wait("sync", self.sem_dma["scalar"], bs_done[0])
            for dc in range(NDC):
                self.dma(dsth[:, dc * 512:dc * 512 + n], sh[dc][:, o:o + n])
                self.dma(dstl[:, dc * 512:dc * 512 + n], sl_[dc][:, o:o + n])

        for q in range(NQ):
            ln2_gp = {}
            for j in range(NTQ):
                tc = q * NTQ + j
                buf = j % 2
                if j >= 2:
                    self.wait("sync", sdv)
                self.dma(x2t[buf], x2_t[tc])
                self.wait("vector", sd)
                if j >= 1:
                    self.wait("vector", spe)
                mv = stats[:, 12:14]
                self.tt("vector", wst, x2t[buf], x2t[buf], ALU.mult)
                self.red("vector", stats[:, 0:1], x2t[buf], ALU.add)
                self.red("vector", stats[:, 1:2], wst, ALU.add)
                self.ts("vector", mv[:, 0:1], stats[:, 0:1], 1.0 / D, ALU.mult)
                self.ts("vector", stats[:, 2:3], stats[:, 1:2], 1.0 / D, ALU.mult)
                self.tt("vector", stats[:, 3:4], mv[:, 0:1], mv[:, 0:1], ALU.mult)
                self.tt("vector", mv[:, 1:2], stats[:, 2:3], stats[:, 3:4],
                        ALU.subtract, inc=sdv)
                s0 = stats[:, 14:15]; r0 = stats[:, 15:16]
                t1 = stats[:, 16:17]; va = stats[:, 17:18]
                self.wait("scalar", sdv)
                self.act(s0, mv[:, 1:2], AF_.Sqrt, bias=self.cst[:, 0:1], inc=sac)
                self.wait("vector", sac)
                self.ts("vector", va, mv[:, 1:2], EPS, ALU.add)
                self.rcp("vector", r0, s0)
                for _ in range(2):
                    self.tt("vector", t1, r0, r0, ALU.mult)
                    self.tt("vector", t1, t1, va, ALU.mult)
                    self.ts("vector", t1, t1, -0.5, ALU.mult, 1.5, ALU.add)
                    self.tt("vector", r0, r0, t1, ALU.mult)
                self.ts("vector", wst, x2t[buf], mv[:, 0:1], ALU.subtract,
                        r0, ALU.mult)
                self.tt("vector", wst, wst, self.CB, ALU.mult)
                self.tt("vector", wst, wst, self.SBt, ALU.add, inc=sdv)
                if tc == 0:
                    self.tap("h2in0", wst, (128, D), wait_sem=sdv)
                gq = self.quant_pair(((wst[:, 0:512], 512),
                                      (wst[:, 512:1024], 512)),
                                     qb, inc=sdv, waits=((sdv, sdv.n),))
                ln2_gp[j] = gq
                if tc == 0 and "hqd" in self.taps:
                    self.wait("sync", sdv)
                    self.wait("sync", sgp, gq)
                    self.tap("hqd", wst, (128, D))
                    self.wait("vector", self.sem_dma["sync"])
                if tc == 0:
                    self.wait("sync", sgp, gq)
                    self.tap("hq0", wst, (128, D), wait_sem=sdv)
                self.wait("tensor", sgp, gq)
                self.tr_split(wst, NDC, hqT_h, hqT_l, TQ, j * 128)

            def gu_tail(gj, j, o, n, silu_hist, su_hist, qnt_hist,
                        gmm_hist, umm_hist):
                n_ = n
                gp = self.ps[(j % 2)][:, 0:n_]
                up = self.ps[2 + (j % 2)][:, 0:n_]
                sg = sg2[j % 2]; su = su2[j % 2]; h2t = h2t2[j % 2]
                self.wait("scalar", spe, gmm_hist[gj])
                self.wait("scalar", sdv, qnt_hist.get(gj - 2, 0))
                self.act(sg[:, 0:n_], gp, AF_.Silu, inc=sac)
                silu_hist[gj] = sac.n
                self.wait("scalar", spe, umm_hist[gj])
                self.acp(su[:, 0:n_], up)
                self.mark("scalar", sac)
                suv = sac.n
                self.wait("vector", sac, suv)
                self.wait("vector", spe, tr_hist.get(gj - 2, 0))
                self.tt("vector", h2t[:, 0:n_], sg[:, 0:n_], su[:, 0:n_],
                        ALU.mult, inc=sdv)
                su_hist[gj] = sdv.n
                hn = (n_ // 32) * 16
                gq = self.quant_pair(((h2t[:, 0:hn], hn),
                                      (h2t[:, hn:n_], n_ - hn)),
                                     qb, inc=sdv, waits=((sdv, su_hist[gj]),))
                qnt_hist[gj] = sdv.n
                qnt_gp[gj] = gq

            def trs_tail(gj, j, o, n):
                nct = n // 128
                pbk = self.ps[4 + (j % 2)]
                self.wait("tensor", sdv, qnt_hist[gj])
                self.wait("tensor", sgp, qnt_gp[gj])
                self.wait("tensor", sdv, trevac_hist.get(gj - 2, 0))
                for jj in range(nct):
                    self.tr(pbk[:, jj * 128:(jj + 1) * 128],
                            h2t2[j % 2][:, jj * 128:(jj + 1) * 128])
                self.mark("tensor", spe)
                tr_hist[gj] = spe.n
                ic0 = o // 128
                hv = h2T_h.rearrange("p (c t) -> p c t", t=TQ)[
                    :, ic0:ic0 + nct, j * 128:(j + 1) * 128]
                lv = h2T_l.rearrange("p (c t) -> p c t", t=TQ)[
                    :, ic0:ic0 + nct, j * 128:(j + 1) * 128]
                pv = pbk[:, 0:nct * 128].rearrange("p (c t) -> p c t", t=128)
                self.wait("scalar", spe, tr_hist[gj])
                self.acp(hv, pv)
                self.mark("scalar", sac)
                self.wait("vector", sac)
                self.tt("vector", lv, pv, hv, ALU.subtract, inc=sdv)
                trevac_hist[gj] = sdv.n

            silu_hist = {}; su_hist = {}; qnt_hist = {}; qnt_gp = {}
            tr_hist = {}; trevac_hist = {}
            gmm_hist = {}; umm_hist = {}
            if q == 0:
                for it, (o, n) in enumerate(itiles):
                    self.wait("sync", spe)
                    self.wait("sync", sdv)
                    self.wait("vector", spe)
                    build_slots(o, n)
                    self.wait("tensor", sdv)
                    for j in range(NTQ):
                        gj = it * NTQ + j
                        gp = self.ps[(j % 2)][:, 0:n]
                        up = self.ps[2 + (j % 2)][:, 0:n]
                        if gj >= 2:
                            self.wait("tensor", sac, silu_hist.get(gj - 2, 0))
                            self.wait("tensor", sdv, su_hist.get(gj - 2, 0))
                        for wi, (wh, wl, pb, hist) in enumerate((
                                (wsg_h, wsg_l, gp, gmm_hist),
                                (wsu_h, wsu_l, up, umm_hist))):
                            for p in range(3):
                                for dc in range(NDC):
                                    lh = (hqT_h if p != 1 else hqT_l)[
                                        :, dc * TQ + j * 128:dc * TQ + (j + 1) * 128]
                                    rh = (wh if p != 2 else wl)[:, dc * 512:
                                                                dc * 512 + n]
                                    self.mm(pb, lh, rh,
                                            start=(p == 0 and dc == 0),
                                            stop=(p == 2 and dc == NDC - 1),
                                            inc=(spe if p == 2 and dc == NDC - 1
                                                 else None))
                            hist[gj] = spe.n
                        gu_tail(gj, j, o, n, silu_hist, su_hist, qnt_hist,
                                gmm_hist, umm_hist)
                        trs_tail(gj, j, o, n)
                    if wd_jobs:
                        self.wait("sync", sdv)
                        for _ in range(5):
                            if wd_jobs:
                                wd_job(*wd_jobs.pop(0))
                self.wait("sync", sdv)
                while wd_jobs:
                    wd_job(*wd_jobs.pop(0))
                wd_st["done"] = self.sem_dma["scalar"].n
            else:
                self.wait("sync", spe)
                self.wait("sync", sd)
                load_wslot(swg_h, swg_l, wsg_h, wsg_l, *itiles[0][:2])
                load_wslot(swu_h, swu_l, wsu_h, wsu_l, *itiles[0][:2])
                pend = None
                for it, (o, n) in enumerate(itiles):
                    self.wait("tensor", sd)
                    for j in range(NTQ):
                        gj = it * NTQ + j
                        gp = self.ps[(j % 2)][:, 0:n]
                        up = self.ps[2 + (j % 2)][:, 0:n]
                        if gj >= 2:
                            self.wait("tensor", sac, silu_hist.get(gj - 2, 0))
                            self.wait("tensor", sdv, su_hist.get(gj - 2, 0))
                        for wi, (wh, wl, pb, hist) in enumerate((
                                (wsg_h, wsg_l, gp, gmm_hist),
                                (wsu_h, wsu_l, up, umm_hist))):
                            for p in range(3):
                                for dc in range(NDC):
                                    lh = (hqT_h if p != 1 else hqT_l)[
                                        :, dc * TQ + j * 128:dc * TQ + (j + 1) * 128]
                                    rh = (wh if p != 2 else wl)[:, dc * 512:
                                                                dc * 512 + n]
                                    self.mm(pb, lh, rh,
                                            start=(p == 0 and dc == 0),
                                            stop=(p == 2 and dc == NDC - 1),
                                            inc=(spe if p == 2 and dc == NDC - 1
                                                 else None))
                            hist[gj] = spe.n
                        if j == NTQ - 1 and it + 1 < len(itiles):
                            self.wait("sync", spe, umm_hist[gj])
                            o2, n2 = itiles[it + 1]
                            load_wslot(swg_h, swg_l, wsg_h, wsg_l, o2, n2)
                            load_wslot(swu_h, swu_l, wsu_h, wsu_l, o2, n2)
                        if pend is not None:
                            trs_tail(*pend)
                        gu_tail(gj, j, o, n, silu_hist, su_hist, qnt_hist,
                                gmm_hist, umm_hist)
                        pend = (gj, j, o, n)
                if pend is not None:
                    trs_tail(*pend)
            if q == 0:
                self.tap("h2T0", h2T_h[:, 0:TQ], (128, TQ), BF16, wait_sem=sdv)
            self.wait("tensor", sdv)
            self.wait("tensor", sac)
            dmm_hist = {}
            self.wait("sync", self.sem_dma["scalar"], wd_st.get("done", 0))
            for ic in range(NIC):
                slot = ic % 2
                if ic >= 2:
                    self.wait("sync", spe, dmm_hist[ic - 2])
                else:
                    self.wait("sync", spe)
                self.dma(dslot_h[slot], swd_h[ic])
                self.dma(dslot_l[slot], swd_l[ic])
                self.wait("tensor", sd)
                for doutc in range(NDC):
                    pb = self.ps[doutc][:, 0:TQ]
                    for p in range(3):
                        lh = (dslot_h[slot] if p != 2 else dslot_l[slot])[
                            :, doutc * 128:(doutc + 1) * 128]
                        rh = (h2T_h if p != 1 else h2T_l)[:, ic * TQ:(ic + 1) * TQ]
                        self.mm(pb, lh, rh, start=(ic == 0 and p == 0),
                                stop=(ic == NIC - 1 and p == 2),
                                inc=(spe if p == 2 and doutc == NDC - 1 else None))
                dmm_hist[ic] = spe.n
            self.wait("vector", spe)
            for doutc in range(NDC):
                self.ts("vector", yT[:, doutc * TQ:(doutc + 1) * TQ],
                        self.ps[doutc][:, 0:TQ],
                        self.vslot(self.VGB)[:, doutc:doutc + 1], ALU.mult,
                        inc=sdv)
            if q == 0:
                self.tap("yT0", yT[:, 0:TQ], (128, TQ), wait_sem=sdv)

            for j in range(NTQ):
                tc = q * NTQ + j
                buf = j % 2
                self.wait("sync", sdv)
                self.dma(x2t[buf], x2_t[tc])
                self.wait("vector", sd)
                self.wait("tensor", sdv)
                for grp in range(2):
                    pb = self.ps[4 + grp]
                    if grp:
                        self.wait("tensor", sdv)
                    for jj in range(4):
                        dc = grp * 4 + jj
                        self.tr(pb[:, jj * 128:(jj + 1) * 128],
                                yT[:, dc * TQ + j * 128:dc * TQ + (j + 1) * 128])
                    self.penop(spe)
                    self.wait("vector", spe)
                    for jj in range(4):
                        dc = grp * 4 + jj
                        dsl = slice(dc * 128, (dc + 1) * 128)
                        self.tt("vector", x2t[buf][:, dsl], x2t[buf][:, dsl],
                                pb[:, jj * 128:(jj + 1) * 128], ALU.add)
                    self.mark("vector", sdv)
                self.wait("sync", sdv)
                self.dma(out_t[tc], x2t[buf])
            self.wait("tensor", sdv)
        self.AF.pop(); self.AB.pop()


from concourse.bass_utils import run_bass_kernel_spmd

_CACHE = {}

def _get_nc():
    if "nc" not in _CACHE:
        mk = MK(S=2048)
        _CACHE["nc"] = mk.build()
    return _CACHE["nc"]


def kernel(L0_attn_x, L0_attn_Wq, L0_attn_bq, L0_attn_Wl, L0_attn_bl,
           L0_attn_Wk, L0_attn_bk, L0_attn_Wv, L0_attn_bv, L0_attn_Wo,
           L0_attn_bo, L0_ffn_w_gate, L0_ffn_w_up, L0_ffn_w_down, L0_c,
           L0_wmod):
    import ml_dtypes
    nc = _get_nc()
    B = L0_attn_x.shape[0]
    cid = np.eye(128, dtype=np.float32)
    cidb = np.eye(128, dtype=ml_dtypes.bfloat16)
    a = np.ascontiguousarray
    shared = {
        "wmod": a(L0_wmod, dtype=np.float32),
        "wq": a(L0_attn_Wq, dtype=np.float32),
        "bq": a(L0_attn_bq, dtype=np.float32).reshape(1, -1),
        "wl": a(L0_attn_Wl, dtype=np.float32),
        "bl": a(L0_attn_bl, dtype=np.float32).reshape(1, -1),
        "wk": a(L0_attn_Wk, dtype=np.float32),
        "bk": a(L0_attn_bk, dtype=np.float32).reshape(1, -1),
        "wv": a(L0_attn_Wv, dtype=np.float32),
        "bv": a(L0_attn_bv, dtype=np.float32).reshape(1, -1),
        "wo": a(L0_attn_Wo, dtype=np.float32),
        "bo": a(L0_attn_bo, dtype=np.float32).reshape(1, -1),
        "wg": a(L0_ffn_w_gate, dtype=np.float32),
        "wu": a(L0_ffn_w_up, dtype=np.float32),
        "wd": a(L0_ffn_w_down, dtype=np.float32),
        "cid": cid, "cidb": cidb,
    }
    in_maps = []
    for b in range(B):
        m = dict(shared)
        m["x"] = a(L0_attn_x[b], dtype=np.float32)
        m["c"] = a(L0_c[b], dtype=np.float32).reshape(1, -1)
        in_maps.append(m)
    res = run_bass_kernel_spmd(nc, in_maps, core_ids=list(range(B)))
    return np.stack([res.results[b]["outp"] for b in range(B)], axis=0)
